# version 02 (final)
# baseline (speedup 1.0000x reference)
import contextlib
import numpy as np
import concourse.bass as bass
import concourse.mybir as mybir
from concourse.bass_utils import run_bass_kernel_spmd

F32 = mybir.dt.float32
BF16 = mybir.dt.bfloat16
AF = mybir.ActivationFunctionType
ALU = mybir.AluOpType
AX = mybir.AxisListType.X

HID = 2048
NH = HID // 128
I_M = 4096
H = 64
P = 64
NST = 128
K_CONV = 4
CONV_DIM = I_M + 2 * NST
PROJ = I_M + CONV_DIM + H
NPROJ = 67
I_F = 1408
NE = 16
SI = 2816
EPS = 1e-5

FFN_T = 1024
MAM_T = 256
SUB = 128


def _bf16():
    import ml_dtypes
    return ml_dtypes.bfloat16


class Sem:
    def __init__(self, handle, name):
        self.h = handle
        self.name = name
        self.n = 0

    def inc(self, ins, k=1):
        ins.then_inc(self.h, k)
        self.n += k
        return self.n


def safe_inc(eng, sem, ins=None):
    eng.drain()
    return sem.inc(eng.nop())


def wait(eng, sem, val):
    if val <= 0:
        return
    assert val <= sem.n, f"wait({sem.name},{val}) but only {sem.n} emitted"
    eng.wait_ge(sem.h, val)


def build_program(S, parts=("pre", "ffn1", "mamba", "ffn2", "post"), n_exp=NE):
    nc = bass.Bass("TRN2", target_bir_lowering=False, detect_race_conditions=False)

    ctx = {"nc": nc, "S": S, "n_exp": n_exp}
    t = ctx

    t["x_in"] = nc.dram_tensor("x", [S, HID], F32, kind="ExternalInput").ap()
    t["w_in"] = nc.dram_tensor("w_in", [PROJ, HID], F32, kind="ExternalInput").ap()
    t["conv_w"] = nc.dram_tensor("conv_w", [CONV_DIM, K_CONV], F32, kind="ExternalInput").ap()
    t["conv_b"] = nc.dram_tensor("conv_b", [CONV_DIM], F32, kind="ExternalInput").ap()
    t["dt_bias"] = nc.dram_tensor("dt_bias", [H], F32, kind="ExternalInput").ap()
    t["a_log"] = nc.dram_tensor("a_log", [H], F32, kind="ExternalInput").ap()
    t["d_vec"] = nc.dram_tensor("d_vec", [H], F32, kind="ExternalInput").ap()
    t["norm_w"] = nc.dram_tensor("norm_w", [I_M], F32, kind="ExternalInput").ap()
    t["w_out"] = nc.dram_tensor("w_out", [HID, I_M], F32, kind="ExternalInput").ap()
    t["gate_w"] = nc.dram_tensor("gate_w", [NE, HID], F32, kind="ExternalInput").ap()
    t["w_gu"] = nc.dram_tensor("w_gate_up", [NE, HID, 2 * I_F], F32, kind="ExternalInput").ap()
    t["w_dn"] = nc.dram_tensor("w_down", [NE, I_F, HID], F32, kind="ExternalInput").ap()
    t["w_sg"] = nc.dram_tensor("shared_gate_up", [HID, 2 * SI], F32, kind="ExternalInput").ap()
    t["w_sd"] = nc.dram_tensor("shared_down", [SI, HID], F32, kind="ExternalInput").ap()
    t["out"] = nc.dram_tensor("out", [S, HID], F32, kind="ExternalOutput").ap()

    t["xfm"] = nc.dram_tensor("xfm", [HID, S], F32, kind="Internal").ap()
    t["wgu_s"] = nc.dram_tensor("wgu_s", [NE, 22, 128, 16, 128], BF16, kind="Internal").ap()
    t["wd_s"] = nc.dram_tensor("wd_s", [NE, 16, 128, 11, 128], BF16, kind="Internal").ap()
    t["wsg_s"] = nc.dram_tensor("wsg_s", [44, 128, 16, 128], BF16, kind="Internal").ap()
    t["wsd_s"] = nc.dram_tensor("wsd_s", [16, 128, 22, 128], BF16, kind="Internal").ap()
    t["win_s"] = nc.dram_tensor("win_s", [NPROJ, 128, 16, 128], BF16, kind="Internal").ap()
    t["wout_s"] = nc.dram_tensor("wout_s", [16, 128, 32, 128], BF16, kind="Internal").ap()

    cN = lambda a, n: nc.inline_tensor(np.ascontiguousarray(a), name=n)
    t["c_ident"] = cN(np.eye(128, dtype=np.float32), "c_ident")
    t["c_ident_bf"] = cN(np.eye(128, dtype=np.float32).astype(_bf16()), "c_ident_bf")
    t["c_ones"] = cN(np.ones((128, 128), np.float32), "c_ones")
    t["c_ones_bf"] = cN(np.ones((128, 128), np.float32).astype(_bf16()), "c_ones_bf")
    sl_mask = -1e9 * (np.arange(128)[:, None] > np.arange(128)[None, :])
    t["c_sl"] = cN(sl_mask.astype(np.float32), "c_sl")
    t["c_m8"] = cN(np.repeat(np.eye(8, dtype=np.float32), 128, axis=1), "c_m8")
    t["c_m8p"] = cN(np.repeat(np.eye(8, dtype=np.float32), P, axis=1), "c_m8p")
    t["c_m16p"] = cN(np.repeat(np.eye(16, dtype=np.float32), P, axis=1), "c_m16p")
    t["c_sel16"] = cN(np.repeat(np.eye(16, dtype=np.float32), 128, axis=1), "c_sel16")
    _KONST_VALS = [EPS, 1.0, 0.0]
    t["_konst_vals"] = _KONST_VALS
    t["c_konst"] = cN(np.tile(np.array(_KONST_VALS, np.float32), (128, 1)), "c_konst")

    with contextlib.ExitStack() as stack:
        sb = lambda name, shape, dt: stack.enter_context(nc.sbuf_tensor(name, shape, dt))
        t["ident"] = sb("ident", [128, 128], F32)
        t["ident_bf"] = sb("ident_bf", [128, 128], BF16)
        t["ones"] = sb("ones", [128, 128], F32)
        t["ones_bf"] = sb("ones_bf", [128, 128], BF16)
        t["slm"] = sb("slm", [128, 128], F32)
        t["sel16"] = sb("sel16", [16, 2048], F32)
        t["gwT"] = sb("gwT", [128, 16, 16], BF16)
        t["nwpp"] = sb("nwpp", [128, 32], F32)
        t["dcol"] = sb("dcol", [128, 32], F32)
        t["acol"] = sb("acol", [64, 1], F32)
        t["dtbcol"] = sb("dtbcol", [64, 1], F32)
        t["cwpp"] = sb("cwpp", [128, 34, 4], F32)
        t["cbpp"] = sb("cbpp", [128, 34], F32)
        t["konst"] = sb("konst", [128, len(t["_konst_vals"])], F32)
        for j, v in enumerate(t["_konst_vals"]):
            nc.const_aps.register(F32, v, t["konst"][:, j:j + 1])

        names = ["pe", "se", "ve", "po",
                 "dwa0", "dwa1", "dwb0", "dwb1", "dwd0", "dwd1",
                 "dx0", "dx1", "dmisc", "dout", "dpre0", "dpre1", "dflat",
                 "dsa0", "dsa1", "dsb0", "dsb1", "bar"]
        t["SM"] = {n: Sem(stack.enter_context(nc.semaphore(n)), n) for n in names}
        t["_bar_round"] = 0

        t["block"] = stack.enter_context(nc.Block())

        emit_const_load(t)
        barrier(t)
        if "pre" in parts:
            emit_prepass_weights(t, ffn=("ffn1" in parts or "ffn2" in parts),
                                 mamba=("mamba" in parts))
            barrier(t)
            emit_prepass_x(t)
            barrier(t)
        if "ffn1" in parts:
            emit_ffn(t)
            barrier(t)
        if "mamba" in parts:
            emit_mamba(t)
            barrier(t)
        if "ffn2" in parts:
            emit_ffn(t)
            barrier(t)
        if "post" in parts:
            emit_post(t)
    return nc


def barrier(ctx):
    block, SM = ctx["block"], ctx["SM"]
    bar = SM["bar"]
    ctx["_bar_round"] += 1
    target = 5 * ctx["_bar_round"]
    for sec in (block.sync, block.tensor, block.scalar, block.vector, block.gpsimd):
        @sec
        def _(eng):
            eng.drain()
            bar.inc(eng.nop())
            eng.wait_ge(bar.h, target)
    assert bar.n == target


def emit_const_load(ctx):
    nc, block, SM = ctx["nc"], ctx["block"], ctx["SM"]
    dm, pe, se, ve = SM["dmisc"], SM["pe"], SM["se"], SM["ve"]

    @block.sync
    def _(sy):
        for dst, src in [(ctx["ident"], ctx["c_ident"]), (ctx["ident_bf"], ctx["c_ident_bf"]),
                         (ctx["ones"], ctx["c_ones"]), (ctx["ones_bf"], ctx["c_ones_bf"]),
                         (ctx["slm"], ctx["c_sl"]), (ctx["sel16"], ctx["c_sel16"]),
                         (ctx["konst"], ctx["c_konst"])]:
            dm.inc(sy.dma_start(out=dst[:], in_=src.ap()[:]), 16)
        dm.inc(sy.dma_start(out=ctx["acol"][:], in_=ctx["a_log"][:, None]), 16)
        dm.inc(sy.dma_start(out=ctx["dtbcol"][:], in_=ctx["dt_bias"][:, None]), 16)
        for c in range(34):
            dm.inc(sy.dma_start(out=ctx["cwpp"][:, c, :],
                                in_=ctx["conv_w"][c * 128:(c + 1) * 128, :]), 16)
            dm.inc(sy.dma_start(out=ctx["cbpp"][:, c:c + 1],
                                in_=ctx["conv_b"][c * 128:(c + 1) * 128, None]), 16)

    @block.scalar
    def _(s):
        wait(s, dm, dm.n)
        s.activation(ctx["acol"][:], ctx["acol"][:], AF.Exp)
        safe_inc(s, se)

    @block.vector
    def _(v):
        wait(v, se, se.n)
        v.tensor_scalar_mul(ctx["acol"][:], ctx["acol"][:], -1.0)
        safe_inc(v, ve)

    with nc.sbuf_tensor("cl_t32", [32, 128], F32) as t32, \
         nc.sbuf_tensor("cl_drow", [1, 64], F32) as drow, \
         nc.sbuf_tensor("cl_gwf", [16, 2048], F32) as gwf, \
         nc.sbuf_tensor("cl_gwb", [16, 2048], BF16) as gwb, \
         nc.psum_tensor("cl_ps", [128, 512], F32) as ps, \
         nc.psum_tensor("cl_pb", [128, 1024], BF16) as pb:
        @block.sync
        def _(sy):
            dm.inc(sy.dma_start(out=t32[:], in_=ctx["norm_w"].rearrange("(c p) -> c p", p=128)), 16)
            dm.inc(sy.dma_start(out=drow[:], in_=ctx["d_vec"][None, :]), 16)
            dm.inc(sy.dma_start(out=gwf[:], in_=ctx["gate_w"][:]), 16)

        @block.vector
        def _(v):
            wait(v, dm, dm.n)
            ve.inc(v.tensor_copy(gwb[:], gwf[:]))

        @block.tensor
        def _(te):
            wait(te, dm, dm.n)
            te.transpose(ps[:, 0:32], t32[:], ctx["ident"][0:32, 0:32])
            pe.inc(te.matmul(ps[:, 128:192], ctx["ones"][0:1, :], drow[:],
                             start=True, stop=True))

        @block.vector
        def _(v):
            wait(v, pe, pe.n)
            v.tensor_copy(ctx["nwpp"][:], ps[:, 0:32])
            v.tensor_copy(ctx["dcol"][0:64, :], ps[0:64, 128:192][:, 0::2])
            v.tensor_copy(ctx["dcol"][64:128, :], ps[64:128, 128:192][:, 1::2])
            safe_inc(v, ve)

        for kc in range(16):
            @block.tensor
            def _(te, kc=kc):
                wait(te, ve, ve.n)
                pe.inc(te.transpose(pb[:, 0:16], gwb[:, kc * 128:(kc + 1) * 128],
                                    ctx["ident_bf"][0:16, 0:16]))

            @block.vector
            def _(v, kc=kc):
                wait(v, pe, pe.n)
                ve.inc(v.tensor_copy(ctx["gwT"][:, kc, :], pb[:, 0:16]))


def emit_prepass_weights(ctx, ffn, mamba):
    nc, block, SM = ctx["nc"], ctx["block"], ctx["SM"]
    pe, ve, po = SM["pe"], SM["ve"], SM["po"]
    dsl = [SM["dpre0"], SM["dpre1"]]
    dout = SM["dout"]

    jobs = []
    if ffn:
        for e in range(ctx["n_exp"]):
            for k in range(16):
                jobs.append((ctx["w_gu"][e, k * 128:(k + 1) * 128, :],
                             ctx["wgu_s"][e, :, :, k, :].rearrange("a p m -> p a m"),
                             22, False))
        for e in range(ctx["n_exp"]):
            for k in range(11):
                jobs.append((ctx["w_dn"][e, k * 128:(k + 1) * 128, :],
                             ctx["wd_s"][e, :, :, k, :].rearrange("a p m -> p a m"),
                             16, False))
        for k in range(16):
            jobs.append((ctx["w_sg"][k * 128:(k + 1) * 128, :],
                         ctx["wsg_s"][:, :, k, :].rearrange("a p m -> p a m"),
                         44, False))
        for k in range(22):
            jobs.append((ctx["w_sd"][k * 128:(k + 1) * 128, :],
                         ctx["wsd_s"][:, :, k, :].rearrange("a p m -> p a m"),
                         16, False))
    if mamba:
        for mo in range(NPROJ):
            rows = 128 if mo < NPROJ - 1 else 64
            jobs.append((ctx["w_in"][mo * 128: mo * 128 + rows, :],
                         (lambda mo=mo: lambda k: ctx["win_s"][mo, :, k, :])(), 16, True))
        for mo in range(16):
            jobs.append((ctx["w_out"][mo * 128:(mo + 1) * 128, :],
                         (lambda mo=mo: lambda k: ctx["wout_s"][mo, :, k, :])(), 32, True))

    nj = len(jobs)
    with nc.sbuf_tensor("pp_f", [128, 2, 5632], F32) as fin, \
         nc.sbuf_tensor("pp_b", [128, 2, 5632], BF16) as bout, \
         nc.sbuf_tensor("pp_t", [128, 2, 128], BF16) as tev, \
         nc.psum_tensor("pp_ps", [128, 2, 1024], BF16) as pst:
        po_at = [0] * nj
        dout_at = [0] * nj
        ve_at = [0] * nj
        dout_tr = {}
        for ji, (src, dst_fn, n_t, tr) in enumerate(jobs):
            sl = ji % 2
            rows, wfree = src.shape
            on_ve = (not tr) and (ji % 2 == 1)

            @block.sync
            def _(sy, src=src, sl=sl, ji=ji, rows=rows, wfree=wfree):
                if ji >= 2:
                    wait(sy, po, po_at[ji - 2])
                dsl[sl].inc(sy.dma_start(out=fin[0:rows, sl, 0:wfree], in_=src[:]), 16)

            def _cast(eng, sl=sl, ji=ji, rows=rows, wfree=wfree):
                wait(eng, dsl[sl], dsl[sl].n)
                if ji >= 2:
                    if jobs[ji - 2][3]:
                        wait(eng, ve, ve_at[ji - 2])
                    else:
                        wait(eng, dout, dout_at[ji - 2])
                po_at[ji] = po.inc(eng.tensor_copy(bout[0:rows, sl, 0:wfree],
                                                   fin[0:rows, sl, 0:wfree]))

            if on_ve:
                block.vector(_cast)
            else:
                block.gpsimd(_cast)

            @block.gpsimd
            def _(gp, sl=sl, ji=ji, rows=rows, wfree=wfree, tr=tr, n_t=n_t,
                  dst_fn=dst_fn, on_ve=on_ve):
                if not tr:
                    wait(gp, po, po_at[ji])
                    dout.inc(gp.dma_start(
                        out=dst_fn,
                        in_=bout[0:rows, sl, 0:wfree].rearrange(
                            "p (a m) -> p a m", a=n_t)), 16)
                    dout_at[ji] = dout.n

            if tr:
                for k in range(n_t):
                    @block.tensor
                    def _(te, sl=sl, k=k, rows=rows, ji=ji):
                        if k == 0:
                            wait(te, po, po_at[ji])
                        wait(te, ve, ve.n)
                        pe.inc(te.transpose(pst[:, k % 2, 0:rows],
                                            bout[0:rows, sl, k * 128:(k + 1) * 128],
                                            ctx["ident_bf"][0:rows, 0:rows]))

                    @block.vector
                    def _(v, sl=sl, k=k, ji=ji, rows=rows):
                        wait(v, pe, pe.n)
                        if (ji, k - 2) in dout_tr:
                            wait(v, dout, dout_tr[(ji, k - 2)])
                        elif k < 2 and ji >= 2 and jobs[ji - 2][3]:
                            wait(v, dout, dout_at[ji - 2])
                        ve_at[ji] = ve.inc(v.tensor_copy(tev[:, k % 2, 0:rows],
                                                         pst[:, k % 2, 0:rows]))

                    @block.gpsimd
                    def _(gp, ji=ji, k=k, dst_fn=dst_fn, rows=rows):
                        wait(gp, ve, ve.n)
                        dout.inc(gp.dma_start(out=dst_fn(k)[:, 0:rows],
                                              in_=tev[:, k % 2, 0:rows]), 16)
                        dout_tr[(ji, k)] = dout.n
                dout_at[ji] = dout.n

        @block.gpsimd
        def _(gp):
            wait(gp, dout, dout.n)


def emit_prepass_x(ctx):
    nc, block, SM = ctx["nc"], ctx["block"], ctx["SM"]
    pe, ve = SM["pe"], SM["ve"]
    dsl = [SM["dx0"], SM["dx1"]]
    dout = SM["dout"]
    S = ctx["S"]
    nt = S // 128
    with nc.sbuf_tensor("px_in", [128, 2, HID], F32) as xin, \
         nc.sbuf_tensor("px_ev", [128, 2, 128], F32) as xev, \
         nc.psum_tensor("px_ps", [128, 2, 512], F32) as pst:
        dout_ev = {}
        for tt in range(nt):
            sl = tt % 2

            @block.sync
            def _(sy, tt=tt, sl=sl):
                if tt >= 2:
                    wait(sy, pe, pe.n)
                dsl[sl].inc(sy.dma_start(out=xin[:, sl, :],
                                         in_=ctx["x_in"][tt * 128:(tt + 1) * 128, :]), 16)
            for fc in range(NH):
                @block.tensor
                def _(te, tt=tt, sl=sl, fc=fc):
                    if fc == 0:
                        wait(te, dsl[sl], dsl[sl].n)
                    wait(te, ve, ve.n)
                    pe.inc(te.transpose(pst[:, fc % 2, 0:128],
                                        xin[:, sl, fc * 128:(fc + 1) * 128],
                                        ctx["ident"][:]))

                @block.vector
                def _(v, tt=tt, fc=fc):
                    wait(v, pe, pe.n)
                    key = (tt * NH + fc) - 2
                    if key >= 0 and key in dout_ev:
                        wait(v, dout, dout_ev[key])
                    ve.inc(v.tensor_copy(xev[:, fc % 2, :], pst[:, fc % 2, 0:128]))

                @block.gpsimd
                def _(gp, tt=tt, fc=fc):
                    wait(gp, ve, ve.n)
                    dout.inc(gp.dma_start(
                        out=ctx["xfm"][fc * 128:(fc + 1) * 128, tt * 128:(tt + 1) * 128],
                        in_=xev[:, fc % 2, :]), 16)
                    dout_ev[tt * NH + fc] = dout.n

        @block.gpsimd
        def _(gp):
            wait(gp, dout, dout.n)


_FFN_UID = [0]


def emit_ffn(ctx):
    nc = ctx["nc"]
    S = ctx["S"]
    u = _FFN_UID[0]
    _FFN_UID[0] += 1
    T = FFN_T
    with nc.sbuf_tensor(f"ff{u}_xn", [128, NH, T], BF16) as xn, \
         nc.sbuf_tensor(f"ff{u}_y", [128, NH, T], F32) as y, \
         nc.sbuf_tensor(f"ff{u}_h", [128, 11, T], BF16) as hbuf, \
         nc.sbuf_tensor(f"ff{u}_xb", [128, 2, T], F32) as xbuf, \
         nc.sbuf_tensor(f"ff{u}_sq", [128, T], BF16) as xsq, \
         nc.sbuf_tensor(f"ff{u}_tg", [128, 2, T], F32) as tmpg, \
         nc.sbuf_tensor(f"ff{u}_th", [128, T], F32) as tmph, \
         nc.sbuf_tensor(f"ff{u}_c", [128, T], F32) as csb, \
         nc.sbuf_tensor(f"ff{u}_wg", [128, 4, 16, 128], BF16) as wgu, \
         nc.sbuf_tensor(f"ff{u}_wd", [128, 2, 11, 128], BF16) as wd, \
         nc.sbuf_tensor(f"ff{u}_rw", [1, 3, T], F32) as rows, \
         nc.sbuf_tensor(f"ff{u}_lg", [16, T], F32) as lgsb, \
         nc.sbuf_tensor(f"ff{u}_cm", [128, 8, 16], F32) as ctm, \
         nc.sbuf_tensor(f"ff{u}_wk", [128, 8, 24], F32) as wk, \
         nc.sbuf_tensor(f"ff{u}_cf", [16, T], F32) as cfm, \
         nc.psum_tensor(f"ff{u}_pA", [128, T], F32) as psA, \
         nc.psum_tensor(f"ff{u}_pB", [128, T], F32) as psB, \
         nc.psum_tensor(f"ff{u}_pC", [128, T], F32) as psC, \
         nc.psum_tensor(f"ff{u}_pD", [128, T], F32) as psD:
        bufs = dict(xn=xn, y=y, hbuf=hbuf, xbuf=xbuf, xsq=xsq, tmpg=tmpg,
                    tmph=tmph, csb=csb, wgu=wgu, wd=wd, rows=rows,
                    lgsb=lgsb, ctm=ctm, wk=wk, cfm=cfm,
                    psA=psA, psB=psB, psC=psC, psD=psD)
        if ctx.get("dbg") and u == 0:
            bufs["snap0_ap"] = nc.dram_tensor("dbg_snap0", [128, T], F32,
                                              kind="ExternalOutput").ap()
            bufs["snapcf_ap"] = nc.dram_tensor("dbg_sncf", [16, T], F32,
                                               kind="ExternalOutput").ap()
            bufs["snapctm_ap"] = nc.dram_tensor("dbg_snctm", [128, 8, 16], F32,
                                                kind="ExternalOutput").ap()
            bufs["snaplg_ap"] = nc.dram_tensor("dbg_snlg", [16, T], F32,
                                               kind="ExternalOutput").ap()
            bufs["snapA_ap"] = nc.dram_tensor("dbg_snA", [128, 8, 16], F32,
                                              kind="ExternalOutput").ap()
            bufs["snapB_ap"] = nc.dram_tensor("dbg_snB", [128, 8, 16], F32,
                                              kind="ExternalOutput").ap()
        for ti in range(S // T):
            _ffn_tile(ctx, ti * T, bufs)
        if ctx.get("dbg") and u == 0:
            nc_, block, SM = ctx["nc"], ctx["block"], ctx["SM"]
            dout = SM["dout"]
            dbg_specs = [("dbg_xn", xn, BF16), ("dbg_y", y, F32),
                         ("dbg_cfm", cfm, F32), ("dbg_rows", rows, F32),
                         ("dbg_lg", lgsb, F32), ("dbg_ctm", ctm, F32),
                         ("dbg_gwT", ctx["gwT"], BF16),
                         ("dbg_identbf", ctx["ident_bf"], BF16),
                         ("dbg_onesbf", ctx["ones_bf"], BF16),
                         ("dbg_sel16", ctx["sel16"], F32),
                         ("dbg_konst", ctx["konst"], F32),
                         ("dbg_nwpp", ctx["nwpp"], F32),
                         ("dbg_dcol", ctx["dcol"], F32),
                         ("dbg_csb", csb, F32), ("dbg_h", hbuf, BF16)]
            dbg_aps = {}
            for nm, src, dt in dbg_specs:
                dbg_aps[nm] = nc_.dram_tensor(nm, list(src.shape), dt,
                                              kind="ExternalOutput").ap()

            dbg_wgu = nc_.dram_tensor("dbg_wgu", [16, 128, 128], BF16,
                                      kind="ExternalOutput").ap()
            dbg_wd = nc_.dram_tensor("dbg_wd", [11, 128, 128], BF16,
                                     kind="ExternalOutput").ap()

            @block.gpsimd
            def _(gp):
                for s in ("pe", "se", "ve", "po"):
                    wait(gp, SM[s], SM[s].n)
                for nm, src, dt in dbg_specs:
                    dout.inc(gp.dma_start(out=dbg_aps[nm][:], in_=src[:]), 16)
                dout.inc(gp.dma_start(out=dbg_wgu[:], in_=ctx["wgu_s"][0, 0]), 16)
                dout.inc(gp.dma_start(out=dbg_wd[:], in_=ctx["wd_s"][0, 0]), 16)
                wait(gp, dout, dout.n)


def _ffn_tile(ctx, t0, B):
    _ffn_ln_router(ctx, t0, B)
    if not ctx.get("skip_experts"):
        for e in range(ctx["n_exp"]):
            _ffn_expert(ctx, e, B, first=(e == 0))
    if not ctx.get("skip_shared"):
        _ffn_shared(ctx, B, first=bool(ctx.get("skip_experts")))
    _ffn_residual(ctx, t0, B)


def _ffn_ln_router(ctx, t0, B):
    nc, block, SM = ctx["nc"], ctx["block"], ctx["SM"]
    pe, se, ve, po = SM["pe"], SM["se"], SM["ve"], SM["po"]
    dsl = [SM["dx0"], SM["dx1"]]
    dflat = SM["dflat"]
    T = FFN_T
    xfm = ctx["xfm"]
    xn, xbuf, xsq, rows = B["xn"], B["xbuf"], B["xsq"], B["rows"]
    psA, psB, psC, psD = B["psA"], B["psB"], B["psC"], B["psD"]
    psS = psA[0:1, :]
    psQ = psB[0:1, :]
    psR = psB
    psM = psC
    psL = psD[0:16, :]

    ve_cast = [0] * NH
    se_sq = [0] * NH
    for ci in range(NH):
        sl = ci % 2

        @block.sync
        def _(sy, ci=ci, sl=sl):
            if ci >= 2:
                wait(sy, se, se_sq[ci - 2])
            else:
                wait(sy, ve, ve.n)
            dsl[sl].inc(sy.dma_start(out=xbuf[:, sl, :],
                                     in_=xfm[ci * 128:(ci + 1) * 128, t0:t0 + T]), 16)

        @block.vector
        def _(v, ci=ci, sl=sl):
            wait(v, dsl[sl], dsl[sl].n)
            if ci < 2:
                wait(v, pe, pe.n)
            ve_cast[ci] = ve.inc(v.tensor_copy(xn[:, ci, :], xbuf[:, sl, :]))

        @block.scalar
        def _(s, ci=ci):
            wait(s, ve, ve_cast[ci])
            if ci >= 1:
                wait(s, pe, pe.n)
            se_sq[ci] = se.inc(s.activation(xsq[:], xn[:, ci, :], AF.Square))

        @block.tensor
        def _(te, ci=ci):
            wait(te, ve, ve_cast[ci])
            wait(te, se, se_sq[ci])
            ins = None
            for hf in range(2):
                sli = slice(hf * 512, (hf + 1) * 512)
                te.matmul(psS[:, sli], ctx["ones_bf"][:, 0:1], xn[:, ci, sli],
                          start=(ci == 0), stop=(ci == NH - 1))
                ins = te.matmul(psQ[:, sli], ctx["ones_bf"][:, 0:1], xsq[:, sli],
                                start=(ci == 0), stop=(ci == NH - 1))
            pe.inc(ins)

    @block.vector
    def _(v):
        wait(v, pe, pe.n)
        v.tensor_scalar_mul(rows[:, 0, :], psS[:], 1.0 / HID)
        v.tensor_scalar_mul(rows[:, 1, :], psQ[:], 1.0 / HID)
        v.drain()
        v.tensor_mul(rows[:, 2, :], rows[:, 0, :], rows[:, 0, :])
        v.drain()
        v.tensor_sub(rows[:, 1, :], rows[:, 1, :], rows[:, 2, :])
        safe_inc(v, ve)

    @block.scalar
    def _(s):
        wait(s, ve, ve.n)
        s.activation(rows[:, 2, :], rows[:, 1, :], AF.Ln, bias=EPS)
        s.drain()
        s.activation(rows[:, 1, :], rows[:, 2, :], AF.Exp, scale=-0.5)
        safe_inc(s, se)

    @block.vector
    def _(v):
        wait(v, se, se.n)
        v.tensor_mul(rows[:, 2, :], rows[:, 0, :], rows[:, 1, :])
        safe_inc(v, ve)

    @block.tensor
    def _(te):
        wait(te, ve, ve.n)
        ins = None
        for hf in range(2):
            sli = slice(hf * 512, (hf + 1) * 512)
            te.matmul(psR[:, sli], ctx["ones"][0:1, :], rows[:, 1, sli],
                      start=True, stop=True)
            ins = te.matmul(psM[:, sli], ctx["ones"][0:1, :], rows[:, 2, sli],
                            start=True, stop=True)
        pe.inc(ins)

    ve_xn = [0] * NH
    for ci in range(NH):
        @block.vector
        def _(v, ci=ci):
            wait(v, pe, pe.n)
            v.tensor_mul(xn[:, ci, :], xn[:, ci, :], psR[:])
            ve_xn[ci] = ve.inc(v.tensor_sub(xn[:, ci, :], xn[:, ci, :], psM[:]))

        @block.tensor
        def _(te, ci=ci):
            wait(te, ve, ve_xn[ci])
            ins = None
            for hf in range(2):
                sli = slice(hf * 512, (hf + 1) * 512)
                ins = te.matmul(psL[:, sli], ctx["gwT"][:, ci, :], xn[:, ci, sli],
                                start=(ci == 0), stop=(ci == NH - 1))
            pe.inc(ins)

    @block.scalar
    def _(s):
        wait(s, pe, pe.n)
        s.copy(B["lgsb"][:], psL[:])
        safe_inc(s, se)

    _softmax_top2(ctx, B)


def _softmax_top2(ctx, B):
    nc, block, SM = ctx["nc"], ctx["block"], ctx["SM"]
    pe, se, ve = SM["pe"], SM["se"], SM["ve"]
    T = FFN_T
    lgsb, ctm, wk, cfm = B["lgsb"], B["ctm"], B["wk"], B["cfm"]
    psT = B["psC"][0:128, 0:128]

    ve_g = [0] * 8
    for g in range(8):
        @block.tensor
        def _(te, g=g):
            wait(te, se, se.n)
            if g >= 1:
                wait(te, ve, ve_g[g - 1])
            pe.inc(te.transpose(psT[:, 0:16], lgsb[:, g * 128:(g + 1) * 128],
                                ctx["ident"][0:16, 0:16]))

        @block.vector
        def _(v, g=g):
            wait(v, pe, pe.n)
            lt = ctm[:, g, :]
            v.tensor_copy(lt, psT[:, 0:16])
            v.drain()
            v.reduce_max(wk[:, g, 16:17], lt, axis=AX)
            v.drain()
            v.tensor_scalar_mul(wk[:, g, 16:17], wk[:, g, 16:17], -1.0)
            safe_inc(v, ve)

        if ctx.get("dbg") and g == 0 and "snapctm_ap" in B:
            @block.gpsimd
            def _(gp):
                wait(gp, SM["ve"], SM["ve"].n)
                SM["dout"].inc(gp.dma_start(out=B["snapctm_ap"][:], in_=ctm[:]), 16)
                SM["dout"].inc(gp.dma_start(out=B["snaplg_ap"][:], in_=lgsb[:]), 16)

        @block.scalar
        def _(s, g=g):
            wait(s, ve, ve.n)
            s.activation(wk[:, g, 0:16], ctm[:, g, :], AF.Exp,
                         bias=wk[:, g, 16:17])
            safe_inc(s, se)

        @block.vector
        def _(v, g=g):
            wait(v, se, se.n)
            ex = wk[:, g, 0:16]
            v.reduce_sum(wk[:, g, 17:18], ex, axis=AX)
            v.drain()
            v.reciprocal(wk[:, g, 18:19], wk[:, g, 17:18])
            v.drain()
            v.tensor_scalar(ctm[:, g, :], ex, wk[:, g, 18:19], None, ALU.mult)
            v.drain()
            v.reduce_max(wk[:, g, 16:17], ctm[:, g, :], axis=AX)
            v.drain()
            v.tensor_scalar(wk[:, g, 0:16], ctm[:, g, :], wk[:, g, 16:17],
                            None, ALU.is_ge)
            v.drain()
            v.scalar_tensor_tensor(wk[:, g, 0:16], wk[:, g, 0:16], -1e30,
                                   ctm[:, g, :], ALU.mult, ALU.add)
            v.drain()
            v.reduce_max(wk[:, g, 19:20], wk[:, g, 0:16], axis=AX)
            v.drain()
            v.tensor_scalar(wk[:, g, 0:16], ctm[:, g, :], wk[:, g, 19:20],
                            None, ALU.is_ge)
            v.drain()
            v.tensor_mul(ctm[:, g, :], ctm[:, g, :], wk[:, g, 0:16])
            ve_g[g] = safe_inc(v, ve)

    se_cf = [0] * 8
    for g in range(8):
        @block.tensor
        def _(te, g=g):
            wait(te, ve, ve_g[g])
            if g >= 1:
                wait(te, se, se_cf[g - 1])
            pe.inc(te.transpose(psT[0:16, 0:128], ctm[:, g, :], ctx["ident"][:]))

        @block.scalar
        def _(s, g=g):
            wait(s, pe, pe.n)
            s.copy(cfm[:, g * 128:(g + 1) * 128], psT[0:16, 0:128])
            se_cf[g] = safe_inc(s, se)
    B["se_cfm_done"] = se_cf[7]


def _ffn_expert(ctx, e, B, first):
    nc, block, SM = ctx["nc"], ctx["block"], ctx["SM"]
    pe, se, ve, po = SM["pe"], SM["se"], SM["ve"], SM["po"]
    dwg = [SM["dwa0"], SM["dwa1"], SM["dwb0"], SM["dwb1"]]
    dwd = [SM["dwd0"], SM["dwd1"]]
    T = FFN_T
    xn, y, hbuf, tmpg, tmph, csb = (B["xn"], B["y"], B["hbuf"], B["tmpg"],
                                    B["tmph"], B["csb"])
    wgu, wd, cfm = B["wgu"], B["wd"], B["cfm"]
    psG = [B["psA"], B["psC"]]
    psU = [B["psB"], B["psD"]]

    if ctx.get("dbg") and first and "snapA_ap" in B:
        @block.gpsimd
        def _(gp):
            wait(gp, se, B["se_cfm_done"])
            SM["dout"].inc(gp.dma_start(out=B["snapA_ap"][:], in_=B["ctm"][:]), 16)

    @block.tensor
    def _(te):
        wait(te, se, B["se_cfm_done"])
        wait(te, ve, ve.n)
        ins = None
        for hf in range(2):
            sli = slice(hf * 512, (hf + 1) * 512)
            ins = te.matmul(psU[0][:, sli],
                            ctx["sel16"][:, e * 128:(e + 1) * 128],
                            cfm[:, sli], start=True, stop=True)
        pe.inc(ins)


    @block.scalar
    def _(s):
        wait(s, pe, pe.n)
        wait(s, po, po.n)
        se.inc(s.copy(csb[:], psU[0][:]))
    se_csb = se.n
    if ctx.get("dbg") and e == 0 and "snap0_ap" in B:
        @block.gpsimd
        def _(gp):
            wait(gp, se, se.n)
            SM["dout"].inc(gp.dma_start(out=B["snap0_ap"][:], in_=csb[:]), 16)
            SM["dout"].inc(gp.dma_start(out=B["snapcf_ap"][:], in_=cfm[:]), 16)

    pe_mp = [0] * 11
    se_mp = [0] * 11
    ve_mp = [0] * 11
    po_mp = [0] * 11
    for mp in range(11):
        slg, slu = 2 * (mp % 2), 2 * (mp % 2) + 1

        @block.sync
        def _(sy, mp=mp, slg=slg, slu=slu):
            if mp >= 2:
                wait(sy, pe, pe_mp[mp - 2])
            else:
                wait(sy, pe, pe.n)
            dwg[slg].inc(sy.dma_start(
                out=wgu[:, slg], in_=ctx["wgu_s"][e, mp]), 16)
            dwg[slu].inc(sy.dma_start(
                out=wgu[:, slu], in_=ctx["wgu_s"][e, mp + 11]), 16)

        @block.tensor
        def _(te, mp=mp, slg=slg, slu=slu):
            wait(te, dwg[slg], dwg[slg].n)
            wait(te, dwg[slu], dwg[slu].n)
            if mp == 0:
                wait(te, se, se_csb)
            if mp >= 2:
                wait(te, se, se_mp[mp - 2])
                wait(te, ve, ve_mp[mp - 2])
            pg, pu = psG[mp % 2], psU[mp % 2]
            ins = None
            for k in range(16):
                for hf in range(2):
                    sli = slice(hf * 512, (hf + 1) * 512)
                    te.matmul(pg[:, sli], wgu[:, slg, k, :], xn[:, k, sli],
                              start=(k == 0), stop=(k == 15))
            for k in range(16):
                for hf in range(2):
                    sli = slice(hf * 512, (hf + 1) * 512)
                    ins = te.matmul(pu[:, sli], wgu[:, slu, k, :], xn[:, k, sli],
                                    start=(k == 0), stop=(k == 15))
            pe_mp[mp] = pe.inc(ins)

        @block.scalar
        def _(s, mp=mp):
            wait(s, pe, pe_mp[mp])
            se_mp[mp] = se.inc(s.activation(tmpg[:, mp % 2, :], psG[mp % 2][:],
                                            AF.Silu))

        @block.vector
        def _(v, mp=mp):
            wait(v, se, se_mp[mp])
            if mp >= 1:
                wait(v, po, po_mp[mp - 1])
            ve_mp[mp] = ve.inc(v.tensor_mul(tmph[:], tmpg[:, mp % 2, :],
                                            psU[mp % 2][:]))

        @block.vector
        def _(v, mp=mp):
            if mp == 0:
                wait(v, se, se_csb)
            po_mp[mp] = po.inc(v.tensor_mul(hbuf[:, mp, :], tmph[:], csb[:]))

    pe_md = [0] * 16
    ve_md = [0] * 16
    for md in range(16):
        sl = md % 2

        @block.sync
        def _(sy, md=md, sl=sl):
            if md >= 2:
                wait(sy, pe, pe_md[md - 2])
            else:
                wait(sy, pe, pe.n)
            dwd[sl].inc(sy.dma_start(
                out=wd[:, sl], in_=ctx["wd_s"][e, md]), 16)

        @block.tensor
        def _(te, md=md, sl=sl):
            wait(te, dwd[sl], dwd[sl].n)
            wait(te, po, po_mp[10])
            if md >= 2:
                wait(te, ve, ve_md[md - 2])
            pg = psG[md % 2]
            ins = None
            for k in range(11):
                for hf in range(2):
                    sli = slice(hf * 512, (hf + 1) * 512)
                    ins = te.matmul(pg[:, sli], wd[:, sl, k, :], hbuf[:, k, sli],
                                    start=(k == 0), stop=(k == 10))
            pe_md[md] = pe.inc(ins)

        @block.vector
        def _(v, md=md, first=first):
            wait(v, pe, pe_md[md])
            if first:
                ve_md[md] = ve.inc(v.tensor_copy(y[:, md, :], psG[md % 2][:]))
            else:
                ve_md[md] = ve.inc(v.tensor_add(y[:, md, :], psG[md % 2][:],
                                                y[:, md, :]))


def _ffn_shared(ctx, B, first=False):
    nc, block, SM = ctx["nc"], ctx["block"], ctx["SM"]
    if ctx.get("dbg") and "snapB_ap" in B:
        @block.gpsimd
        def _(gp):
            for snm in ("pe", "se", "ve", "po"):
                wait(gp, SM[snm], SM[snm].n)
            SM["dout"].inc(gp.dma_start(out=B["snapB_ap"][:], in_=B["ctm"][:]), 16)
    pe, se, ve, po = SM["pe"], SM["se"], SM["ve"], SM["po"]
    dwg = [SM["dwa0"], SM["dwa1"], SM["dwb0"], SM["dwb1"]]
    dwd = [SM["dwd0"], SM["dwd1"]]
    T = FFN_T
    xn, y, hbuf, tmpg = B["xn"], B["y"], B["hbuf"], B["tmpg"]
    wgu, wd = B["wgu"], B["wd"]
    psG = [B["psA"], B["psC"]]
    psU = [B["psB"], B["psD"]]
    for half in range(2):
        pe_mp = [0] * 11
        se_mp = [0] * 11
        ve_mp = [0] * 11
        for j in range(11):
            ks = half * 11 + j
            slg, slu = 2 * (j % 2), 2 * (j % 2) + 1

            @block.sync
            def _(sy, j=j, ks=ks, slg=slg, slu=slu):
                if j >= 2:
                    wait(sy, pe, pe_mp[j - 2])
                else:
                    wait(sy, pe, pe.n)
                dwg[slg].inc(sy.dma_start(
                    out=wgu[:, slg], in_=ctx["wsg_s"][ks]), 16)
                dwg[slu].inc(sy.dma_start(
                    out=wgu[:, slu], in_=ctx["wsg_s"][22 + ks]), 16)

            @block.tensor
            def _(te, j=j, slg=slg, slu=slu):
                wait(te, dwg[slg], dwg[slg].n)
                wait(te, dwg[slu], dwg[slu].n)
                if j >= 2:
                    wait(te, se, se_mp[j - 2])
                    wait(te, ve, ve_mp[j - 2])
                else:
                    wait(te, ve, ve.n)
                pg, pu = psG[j % 2], psU[j % 2]
                ins = None
                for k in range(16):
                    for hf in range(2):
                        sli = slice(hf * 512, (hf + 1) * 512)
                        te.matmul(pg[:, sli], wgu[:, slg, k, :], xn[:, k, sli],
                                  start=(k == 0), stop=(k == 15))
                for k in range(16):
                    for hf in range(2):
                        sli = slice(hf * 512, (hf + 1) * 512)
                        ins = te.matmul(pu[:, sli], wgu[:, slu, k, :], xn[:, k, sli],
                                        start=(k == 0), stop=(k == 15))
                pe_mp[j] = pe.inc(ins)

            @block.scalar
            def _(s, j=j):
                wait(s, pe, pe_mp[j])
                se_mp[j] = se.inc(s.activation(tmpg[:, j % 2, :], psG[j % 2][:],
                                               AF.Silu))

            @block.vector
            def _(v, j=j):
                wait(v, se, se_mp[j])
                ve_mp[j] = ve.inc(v.tensor_mul(hbuf[:, j, :], tmpg[:, j % 2, :],
                                               psU[j % 2][:]))

        pe_md = [0] * 16
        ve_md = [0] * 16
        for md in range(16):
            sl = md % 2

            @block.sync
            def _(sy, md=md, sl=sl, half=half):
                if md >= 2:
                    wait(sy, pe, pe_md[md - 2])
                else:
                    wait(sy, pe, pe.n)
                dwd[sl].inc(sy.dma_start(
                    out=wd[:, sl],
                    in_=ctx["wsd_s"][md, :, half * 11:(half + 1) * 11, :]), 16)

            @block.tensor
            def _(te, md=md, sl=sl):
                wait(te, dwd[sl], dwd[sl].n)
                wait(te, ve, ve_mp[10])
                if md >= 2:
                    wait(te, ve, ve_md[md - 2])
                pg = psG[md % 2]
                ins = None
                for k in range(11):
                    for hf in range(2):
                        sli = slice(hf * 512, (hf + 1) * 512)
                        ins = te.matmul(pg[:, sli], wd[:, sl, k, :], hbuf[:, k, sli],
                                        start=(k == 0), stop=(k == 10))
                pe_md[md] = pe.inc(ins)

            @block.vector
            def _(v, md=md, half=half):
                wait(v, pe, pe_md[md])
                if first and half == 0:
                    ve_md[md] = ve.inc(v.tensor_copy(y[:, md, :], psG[md % 2][:]))
                else:
                    ve_md[md] = ve.inc(v.tensor_add(y[:, md, :], psG[md % 2][:],
                                                    y[:, md, :]))


def _ffn_residual(ctx, t0, B):
    nc, block, SM = ctx["nc"], ctx["block"], ctx["SM"]
    ve = SM["ve"]
    dsl = [SM["dx0"], SM["dx1"]]
    dout = SM["dout"]
    T = FFN_T
    xfm = ctx["xfm"]
    y, xbuf, tmpg = B["y"], B["xbuf"], B["tmpg"]
    ve_res = [0] * NH
    dout_at = [0] * NH
    for ci in range(NH):
        sl = ci % 2

        @block.sync
        def _(sy, ci=ci, sl=sl):
            if ci >= 2:
                wait(sy, ve, ve_res[ci - 2])
            else:
                wait(sy, ve, ve.n)
            dsl[sl].inc(sy.dma_start(out=xbuf[:, sl, :],
                                     in_=xfm[ci * 128:(ci + 1) * 128, t0:t0 + T]), 16)

        @block.vector
        def _(v, ci=ci, sl=sl):
            wait(v, dsl[sl], dsl[sl].n)
            if ci >= 2:
                wait(v, dout, dout_at[ci - 2])
            ve_res[ci] = ve.inc(
                v.scalar_tensor_tensor(tmpg[:, sl, :], y[:, ci, :], 0.5,
                                       xbuf[:, sl, :], ALU.mult, ALU.add))

        @block.gpsimd
        def _(gp, ci=ci, sl=sl):
            wait(gp, ve, ve_res[ci])
            dout.inc(gp.dma_start(out=xfm[ci * 128:(ci + 1) * 128, t0:t0 + T],
                                  in_=tmpg[:, sl, :]), 16)
            dout_at[ci] = dout.n

    @block.gpsimd
    def _(gp):
        wait(gp, dout, dout.n)


def emit_mamba(ctx):
    nc, block, SM = ctx["nc"], ctx["block"], ctx["SM"]
    pe, se, ve, po = SM["pe"], SM["se"], SM["ve"], SM["po"]
    S = ctx["S"]
    T = MAM_T
    with contextlib.ExitStack() as _st:
        ar1 = _st.enter_context(nc.sbuf_tensor("mm_ar1", [128, 16, T], BF16))
        hbc = _st.enter_context(nc.sbuf_tensor("mm_hbc", [128, 34, T], BF16))
        gsil = _st.enter_context(nc.sbuf_tensor("mm_gate", [128, 32, T], BF16))
        yfm = _st.enter_context(nc.sbuf_tensor("mm_yfm", [128, 32, T], BF16))
        xdt = _st.enter_context(nc.sbuf_tensor("mm_xdt", [128, 4096], BF16))
        xtm = _st.enter_context(nc.sbuf_tensor("mm_xtm", [128, 4096], BF16))
        ytm = _st.enter_context(nc.sbuf_tensor("mm_ytm", [128, 4096], BF16))
        xdte = _st.enter_context(nc.sbuf_tensor("mm_xdte", [128, 4096], BF16))
        carry = _st.enter_context(nc.sbuf_tensor("mm_carry", [128, 4096], F32))
        carrybf = _st.enter_context(nc.sbuf_tensor("mm_carrybf", [128, 4096], BF16))
        wband = _st.enter_context(nc.sbuf_tensor("mm_wb", [128, 2, 32, 128], BF16))
        xbuf = _st.enter_context(nc.sbuf_tensor("mm_xb", [128, 2, T], F32))
        xsq = _st.enter_context(nc.sbuf_tensor("mm_sq", [128, T], BF16))
        rows = _st.enter_context(nc.sbuf_tensor("mm_rw", [1, 3, T], F32))
        stg = _st.enter_context(nc.sbuf_tensor("mm_st", [128, 2, T + 3], F32))
        accb = _st.enter_context(nc.sbuf_tensor("mm_acc", [128, 2, T], F32))
        ctail = _st.enter_context(nc.sbuf_tensor("mm_ct", [128, 34, 3], F32))
        zb = _st.enter_context(nc.sbuf_tensor("mm_z", [64, T], F32))
        dtf = _st.enter_context(nc.sbuf_tensor("mm_dt", [64, T], F32))
        dAf = _st.enter_context(nc.sbuf_tensor("mm_dA", [64, T], F32))
        acum = _st.enter_context(nc.sbuf_tensor("mm_ac", [64, T], F32))
        z64 = _st.enter_context(nc.sbuf_tensor("mm_z64", [64, 128], F32))
        s8 = _st.enter_context(nc.sbuf_tensor("mm_s8", [8, 2, 2, 128], F32))
        r8a = _st.enter_context(nc.sbuf_tensor("mm_r8a", [8, 1024], F32))
        r8b = _st.enter_context(nc.sbuf_tensor("mm_r8b", [8, 1024], F32))
        cdm = _st.enter_context(nc.sbuf_tensor("mm_cdm", [8, 512], F32))
        cd8 = _st.enter_context(nc.sbuf_tensor("mm_cd8", [8, 1], F32))
        s16 = _st.enter_context(nc.sbuf_tensor("mm_s16", [16, 2, 128], F32))
        ed16 = _st.enter_context(nc.sbuf_tensor("mm_ed16", [16, 128], F32))
        dte16 = _st.enter_context(nc.sbuf_tensor("mm_dte16", [16, 128], F32))
        mbuf = _st.enter_context(nc.sbuf_tensor("mm_mb", [128, 1024], BF16))
        csc = _st.enter_context(nc.sbuf_tensor("mm_cs", [128, 1024], BF16))
        cbs = _st.enter_context(nc.sbuf_tensor("mm_cbs", [128, 128], BF16))
        cb2 = _st.enter_context(nc.sbuf_tensor("mm_cb2", [128, 128], BF16))
        t2 = _st.enter_context(nc.sbuf_tensor("mm_t2", [128, 2, T], F32))
        psP = _st.enter_context(nc.psum_tensor("mm_pP", [128, 2, 512], F32))
        psT = _st.enter_context(nc.psum_tensor("mm_pT", [128, 2, 1024], BF16))
        psD = _st.enter_context(nc.psum_tensor("mm_pD", [128, 1024], F32))
        psY = _st.enter_context(nc.psum_tensor("mm_pY", [128, 2, 512], F32))
        B = dict(ar1=ar1, hbc=hbc, gsil=gsil, yfm=yfm, xdt=xdt, xdte=xdte,
                 xtm=xtm, ytm=ytm, carrybf=carrybf,
                 carry=carry, wband=wband, xbuf=xbuf, xsq=xsq, rows=rows,
                 stg=stg, accb=accb, ctail=ctail, zb=zb, dtf=dtf, dAf=dAf,
                 acum=acum, z64=z64, s8=s8, r8a=r8a, r8b=r8b, cdm=cdm,
                 cd8=cd8, s16=s16, ed16=ed16, dte16=dte16, mbuf=mbuf,
                 csc=csc, cbs=cbs, cb2=cb2, t2=t2, psP=psP, psT=psT,
                 psD=psD, psY=psY)

        m8 = _st.enter_context(nc.sbuf_tensor("mm_m8", [8, 1024], F32))
        m8p = _st.enter_context(nc.sbuf_tensor("mm_m8p", [8, 512], F32))
        m16p = _st.enter_context(nc.sbuf_tensor("mm_m16p", [16, 1024], F32))
        ctx["m8"], ctx["m8p"], ctx["m16p"] = m8, m8p, m16p
        dm = SM["dmisc"]

        @block.sync
        def _(sy):
            wait(sy, ve, ve.n)
            wait(sy, se, se.n)
            wait(sy, pe, pe.n)
            dm.inc(sy.dma_start(out=m8[:], in_=ctx["c_m8"].ap()[:]), 16)
            dm.inc(sy.dma_start(out=m8p[:], in_=ctx["c_m8p"].ap()[:]), 16)
            dm.inc(sy.dma_start(out=m16p[:], in_=ctx["c_m16p"].ap()[:]), 16)

        B["hist"] = {k: [] for k in ("pe_diff", "pe_bcast", "pe_y", "pe_states",
                                     "ve_pre", "ve_cs", "ve_carry", "ve_yev",
                                     "se_exp")}

        @block.vector
        def _(v):
            wait(v, dm, dm.n)
            v.memset(z64[:], 0.0)
            v.memset(ctail[:], 0.0)
            v.memset(carry[:], 0.0)
            v.memset(carrybf[:], 0.0)
            safe_inc(v, ve)

        for ti in range(S // T):
            _mamba_tile(ctx, B, ti * T)


def _mamba_tile(ctx, B, t0):
    ms = ctx.get("m_skip", 0)
    _mamba_ln(ctx, B, t0)
    _mamba_proj(ctx, B)
    if ms in (0, 2):
        for sc in range(MAM_T // SUB):
            _mamba_ssd_sub(ctx, B, sc)
    if ms == 0:
        _mamba_epilogue(ctx, B, t0)


def _mamba_ln(ctx, B, t0):
    nc, block, SM = ctx["nc"], ctx["block"], ctx["SM"]
    pe, se, ve = SM["pe"], SM["se"], SM["ve"]
    dsl = [SM["dx0"], SM["dx1"]]
    T = MAM_T
    xfm = ctx["xfm"]
    xn, xbuf, xsq, rows = B["ar1"], B["xbuf"], B["xsq"], B["rows"]
    psD, psY = B["psD"], B["psY"]
    psS = psD[0:1, 0:T]
    psQ = psD[0:1, 512:512 + T]
    psR = psD[:, 0:T]
    psM = psY[:, 0, 0:T]

    ve_cast = [0] * NH
    se_sq = [0] * NH
    for ci in range(NH):
        sl = ci % 2

        @block.sync
        def _(sy, ci=ci, sl=sl):
            if ci >= 2:
                wait(sy, se, se_sq[ci - 2])
            else:
                wait(sy, ve, ve.n)
                wait(sy, se, se.n)
            dsl[sl].inc(sy.dma_start(out=xbuf[:, sl, :],
                                     in_=xfm[ci * 128:(ci + 1) * 128, t0:t0 + T]), 16)

        @block.vector
        def _(v, ci=ci, sl=sl):
            wait(v, dsl[sl], dsl[sl].n)
            if ci < 2:
                wait(v, pe, pe.n)
            ve_cast[ci] = ve.inc(v.tensor_copy(xn[:, ci, :], xbuf[:, sl, :]))

        @block.scalar
        def _(s, ci=ci):
            wait(s, ve, ve_cast[ci])
            if ci >= 1:
                wait(s, pe, pe.n)
            se_sq[ci] = se.inc(s.activation(xsq[:], xn[:, ci, :], AF.Square))

        @block.tensor
        def _(te, ci=ci):
            wait(te, ve, ve_cast[ci])
            wait(te, se, se_sq[ci])
            if ci == 0:
                wait(te, ve, ve.n)
            te.matmul(psS[:], ctx["ones_bf"][:, 0:1], xn[:, ci, :],
                      start=(ci == 0), stop=(ci == NH - 1))
            ins = te.matmul(psQ[:], ctx["ones_bf"][:, 0:1], xsq[:],
                            start=(ci == 0), stop=(ci == NH - 1))
            pe.inc(ins)

    @block.vector
    def _(v):
        wait(v, pe, pe.n)
        v.tensor_scalar_mul(rows[:, 0, :], psS[:], 1.0 / HID)
        v.tensor_scalar_mul(rows[:, 1, :], psQ[:], 1.0 / HID)
        v.drain()
        v.tensor_mul(rows[:, 2, :], rows[:, 0, :], rows[:, 0, :])
        v.drain()
        v.tensor_sub(rows[:, 1, :], rows[:, 1, :], rows[:, 2, :])
        safe_inc(v, ve)

    @block.scalar
    def _(s):
        wait(s, ve, ve.n)
        s.activation(rows[:, 2, :], rows[:, 1, :], AF.Ln, bias=EPS)
        s.drain()
        s.activation(rows[:, 1, :], rows[:, 2, :], AF.Exp, scale=-0.5)
        safe_inc(s, se)

    @block.vector
    def _(v):
        wait(v, se, se.n)
        v.tensor_mul(rows[:, 2, :], rows[:, 0, :], rows[:, 1, :])
        safe_inc(v, ve)

    @block.tensor
    def _(te):
        wait(te, ve, ve.n)
        te.matmul(psR[:], ctx["ones"][0:1, :], rows[:, 1, :], start=True, stop=True)
        ins = te.matmul(psM[:], ctx["ones"][0:1, :], rows[:, 2, :], start=True, stop=True)
        pe.inc(ins)

    for ci in range(NH):
        @block.vector
        def _(v, ci=ci):
            wait(v, pe, pe.n)
            v.tensor_mul(xn[:, ci, :], xn[:, ci, :], psR[:])
            v.drain()
            ve.inc(v.tensor_sub(xn[:, ci, :], xn[:, ci, :], psM[:]))
    B["ve_xn_done"] = ve.n


def _mamba_proj(ctx, B):
    nc, block, SM = ctx["nc"], ctx["block"], ctx["SM"]
    pe, se, ve, po = SM["pe"], SM["se"], SM["ve"], SM["po"]
    dw = [SM["dwa0"], SM["dwa1"]]
    T = MAM_T
    xn = B["ar1"]
    psP = B["psP"]
    po_conv = B["po_conv"] = [0] * 34
    pe_mo = [0] * NPROJ
    ev_mo = [0] * NPROJ
    ev_eng = [None] * NPROJ

    for mo in range(NPROJ):
        sl = mo % 2
        rows_m = 128 if mo < NPROJ - 1 else 64

        @block.sync
        def _(sy, mo=mo, sl=sl, rows_m=rows_m):
            if mo >= 2:
                wait(sy, pe, pe_mo[mo - 2])
            else:
                wait(sy, pe, pe.n)
            dw[sl].inc(sy.dma_start(
                out=B["wband"][:, sl, 0:16, 0:rows_m],
                in_=ctx["win_s"][mo][:, :, 0:rows_m]), 16)

        @block.tensor
        def _(te, mo=mo, sl=sl, rows_m=rows_m):
            wait(te, dw[sl], dw[sl].n)
            if mo == 0:
                wait(te, ve, B["ve_xn_done"])
            if mo >= 2:
                eng, cnt = ev_eng[mo - 2], ev_mo[mo - 2]
                wait(te, eng, cnt)
            ins = None
            for k in range(16):
                ins = te.matmul(psP[0:rows_m, sl, 0:T], B["wband"][:, sl, k, 0:rows_m],
                                xn[:, k, :], start=(k == 0), stop=(k == 15))
            pe_mo[mo] = pe.inc(ins)

        if mo < 32:
            @block.scalar
            def _(s, mo=mo, sl=sl):
                wait(s, pe, pe_mo[mo])
                ev_mo[mo] = se.inc(s.activation(B["gsil"][:, mo, :], psP[:, sl, 0:T],
                                                AF.Silu))
            ev_eng[mo] = SM["se"]
        elif mo < 66:
            ci = mo - 32

            @block.vector
            def _(v, mo=mo, sl=sl, ci=ci):
                wait(v, pe, pe_mo[mo])
                if ci >= 2:
                    wait(v, ve, po_conv[ci - 2])
                v.tensor_copy(B["stg"][:, sl, 0:3], B["ctail"][:, ci, :])
                v.drain()
                v.tensor_copy(B["stg"][:, sl, 3:T + 3], psP[:, sl, 0:T])
                v.drain()
                v.tensor_copy(B["ctail"][:, ci, :], B["stg"][:, sl, T:T + 3])
                ev_mo[mo] = safe_inc(v, ve)
            ev_eng[mo] = SM["ve"]

            @block.vector
            def _(v, sl=sl, ci=ci):
                v.tensor_scalar(B["accb"][:, sl, :], B["stg"][:, sl, 0:T],
                                ctx["cwpp"][:, ci, 0:1], None, ALU.mult)
                for k in range(1, 4):
                    v.drain()
                    v.scalar_tensor_tensor(B["accb"][:, sl, :],
                                           B["stg"][:, sl, k:k + T],
                                           ctx["cwpp"][:, ci, k:k + 1],
                                           B["accb"][:, sl, :], ALU.mult, ALU.add)
                po_conv[ci] = safe_inc(v, ve)

            @block.scalar
            def _(s, sl=sl, ci=ci):
                wait(s, ve, po_conv[ci])
                se.inc(s.activation(B["hbc"][:, ci, :], B["accb"][:, sl, :],
                                    AF.Silu, bias=ctx["cbpp"][:, ci:ci + 1]))
        else:
            @block.vector
            def _(v, sl=sl, mo=mo):
                wait(v, pe, pe_mo[mo])
                v.tensor_scalar(B["zb"][:], psP[0:64, sl, 0:T], ctx["dtbcol"][:],
                                None, ALU.add)
                ev_mo[mo] = safe_inc(v, ve)
            ev_eng[mo] = SM["ve"]

            @block.scalar
            def _(s):
                wait(s, ve, ev_mo[mo])
                s.activation(B["dtf"][:], B["zb"][:], AF.Abs)
                s.drain()
                s.activation(B["dtf"][:], B["dtf"][:], AF.Exp, scale=-1.0)
                s.drain()
                s.activation(B["dtf"][:], B["dtf"][:], AF.Ln, bias=1.0)
                safe_inc(s, se)

            @block.vector
            def _(v):
                wait(v, se, se.n)
                v.tensor_relu(B["dAf"][:], B["zb"][:])
                v.drain()
                v.tensor_add(B["dtf"][:], B["dtf"][:], B["dAf"][:])
                v.drain()
                v.tensor_scalar(B["dAf"][:], B["dtf"][:], ctx["acol"][:],
                                None, ALU.mult)
                safe_inc(v, ve)
    B["ve_dt_done"] = ve.n
    B["se_hbc_done"] = se.n


def _mamba_ssd_sub(ctx, B, sc):
    nc, block, SM = ctx["nc"], ctx["block"], ctx["SM"]
    pe, se, ve, po = SM["pe"], SM["se"], SM["ve"], SM["po"]
    dsa = [SM["dsa0"], SM["dsa1"]]
    dsb = [SM["dsb0"], SM["dsb1"]]
    l0 = sc * SUB
    hbc, xdt, xdte, carry = B["hbc"], B["xdt"], B["xdte"], B["carry"]
    psP, psT, psD, psY = B["psP"], B["psT"], B["psD"], B["psY"]
    acum = B["acum"]
    Y_tm = B["ytm"][:]
    X_tm = B["xtm"][:]

    @block.vector
    def _(v):
        wait(v, ve, B["ve_dt_done"])
        if sc == 0:
            wait(v, pe, pe.n)
        v.tensor_tensor_scan(acum[:, l0:l0 + SUB], B["dAf"][:, l0:l0 + SUB],
                             B["z64"][:], 0.0, ALU.add, ALU.add)
        safe_inc(v, ve)
    ve_scan = ve.n
    B["ve_scan"] = ve_scan

    if ctx.get("ssd_skip", 0) >= 3:
        return
    se_tr = 0
    for ci in range(33):
        @block.tensor
        def _(te, ci=ci):
            if ci == 0:
                wait(te, se, B["se_hbc_done"])
                wait(te, se, se.n)
                if sc == 0:
                    wait(te, ve, ve_scan)
            wait(te, se, se.n if ci < 2 else se_tr)
            pe.inc(te.transpose(psT[:, ci % 2, 0:128], hbc[:, ci, l0:l0 + SUB],
                                ctx["ident_bf"][:]))

        @block.scalar
        def _(s, ci=ci):
            nonlocal_se = None
            wait(s, pe, pe.n)
            if ci < 32:
                dst = X_tm[:, ci * 128:(ci + 1) * 128]
            else:
                dst = B["cbs"][:]
            se.inc(s.copy(dst, psT[:, ci % 2, 0:128]))
        se_tr = se.n
    B_tm_sb = B["cbs"]

    if ctx.get("ssd_skip", 0) >= 4:
        return
    ve_x = 0
    for g4 in range(4):
        sl = g4 % 2

        @block.gpsimd
        def _(gp, g4=g4, sl=sl):
            wait(gp, ve, ve_scan)
            wait(gp, pe, pe.n)
            dsb[sl].inc(gp.dma_start(out=B["s16"][:, sl, :],
                                     in_=B["dtf"][g4 * 16:(g4 + 1) * 16, l0:l0 + SUB]), 16)
            dsb[sl].inc(gp.dma_start(out=B["ed16"][:],
                                     in_=acum[g4 * 16:(g4 + 1) * 16, l0:l0 + SUB]), 16)

        @block.scalar
        def _(s, g4=g4, sl=sl):
            wait(s, dsb[sl], dsb[sl].n)
            s.activation(B["dte16"][:], B["ed16"][:], AF.Exp, scale=-1.0,
                         bias=B["ed16"][:, 127:128])
            safe_inc(s, se)

        @block.vector
        def _(v, g4=g4, sl=sl):
            wait(v, se, se.n)
            v.tensor_mul(B["dte16"][:], B["dte16"][:], B["s16"][:, sl, :])
            safe_inc(v, ve)

        @block.tensor
        def _(te, g4=g4, sl=sl):
            wait(te, ve, ve.n)
            if g4 == 0:
                wait(te, se, se.n)
                wait(te, ve, ve.n)
            if g4 >= 1:
                wait(te, ve, ve_x)
            te.matmul(psD[:, 0:512], B["s16"][:, sl, :], ctx["m16p"][:, 0:512],
                      start=True, stop=True)
            te.matmul(psD[:, 512:1024], B["s16"][:, sl, :], ctx["m16p"][:, 512:1024],
                      start=True, stop=True)
            te.matmul(psY[:, 0, :], B["dte16"][:], ctx["m16p"][:, 0:512],
                      start=True, stop=True)
            ins = te.matmul(psY[:, 1, :], B["dte16"][:], ctx["m16p"][:, 512:1024],
                            start=True, stop=True)
            pe.inc(ins)

        @block.vector
        def _(v, g4=g4):
            wait(v, pe, pe.n)
            sli = slice(g4 * 1024, (g4 + 1) * 1024)
            v.tensor_mul(xdt[:, sli], X_tm[:, sli], psD[:, 0:1024])
            v.drain()
            v.tensor_mul(xdte[:, sli][:, 0:512], X_tm[:, sli][:, 0:512], psY[:, 0, :])
            v.drain()
            v.tensor_mul(xdte[:, sli][:, 512:1024], X_tm[:, sli][:, 512:1024],
                         psY[:, 1, :])
            ve_x = safe_inc(v, ve)

    if ctx.get("ssd_skip", 0) >= 5:
        return
    @block.tensor
    def _(te):
        wait(te, se, se_tr)
        wait(te, ve, ve_x)
        pe.inc(te.matmul(psY[:, 0, :][:, 0:128], hbc[:, 32, l0:l0 + SUB],
                         hbc[:, 33, l0:l0 + SUB], start=True, stop=True))

    @block.vector
    def _(v):
        wait(v, pe, pe.n)
        v.tensor_copy(B["cb2"][:], psY[:, 0, :][:, 0:128])
        safe_inc(v, ve)
    se_cb = 0
    ve_cb = ve.n

    if ctx.get("ssd_skip", 0) < 1:
        for g in range(8):
            _mamba_ssd_group(ctx, B, sc, g, B_tm_sb, ve_cb)

    if ctx.get("ssd_skip", 0) >= 2:
        return
    se_y = 0
    for ci in range(32):
        @block.tensor
        def _(te, ci=ci):
            if ci == 0:
                wait(te, ve, ve.n)
                wait(te, se, se.n)
            wait(te, se, se.n if ci < 2 else se_y)
            pe.inc(te.transpose(psT[:, ci % 2, 0:128],
                                Y_tm[:, ci * 128:(ci + 1) * 128],
                                ctx["ident_bf"][:]))

        @block.scalar
        def _(s, ci=ci):
            wait(s, pe, pe.n)
            se.inc(s.copy(B["yfm"][:, ci, l0:l0 + SUB], psT[:, ci % 2, 0:128]))
        se_y = se.n


def _mamba_ssd_group(ctx, B, sc, g, B_tm_sb, ve_cb):
    nc, block, SM = ctx["nc"], ctx["block"], ctx["SM"]
    pe, se, ve, po = SM["pe"], SM["se"], SM["ve"], SM["po"]
    dsa = [SM["dsa0"], SM["dsa1"]]
    l0 = sc * SUB
    sl = g % 2
    psP, psD, psY = B["psP"], B["psD"], B["psY"]
    Y_tm = B["ytm"][:]
    H_ = B["hist"]

    @block.gpsimd
    def _(gp, g=g, sl=sl):
        tgt = B["ve_scan"]
        if len(H_["ve_pre"]) >= 2:
            tgt = max(tgt, H_["ve_pre"][-2])
        wait(gp, ve, tgt)
        if len(H_["pe_diff"]) >= 2:
            wait(gp, pe, H_["pe_diff"][-2])
        dsa[sl].inc(gp.dma_start(out=B["s8"][:, sl, 0, :],
                                 in_=B["acum"][g * 8:(g + 1) * 8, l0:l0 + SUB]), 16)

    @block.vector
    def _(v, g=g, sl=sl):
        wait(v, dsa[sl], dsa[sl].n)
        if H_["pe_diff"]:
            wait(v, pe, H_["pe_diff"][-1])
        A8 = B["s8"][:, sl, 0, :]
        v.tensor_scalar_mul(B["s8"][:, sl, 1, :], A8, -1.0)
        v.drain()
        v.tensor_tensor(B["r8a"].rearrange("p (a b) -> p a b", a=8),
                        ctx["m8"].rearrange("p (a b) -> p a b", a=8),
                        A8[:, None, :].broadcast_to([8, 8, 128]), ALU.mult)
        safe_inc(v, ve)

    @block.scalar
    def _(s, g=g, sl=sl):
        wait(s, dsa[sl], dsa[sl].n)
        if H_["ve_pre"]:
            wait(s, ve, H_["ve_pre"][-1])
        A8 = B["s8"][:, sl, 0, :]
        s.activation(B["ed16"][0:8, :], A8, AF.Exp)
        s.drain()
        s.copy(B["cd8"][:], B["ed16"][0:8, 127:128])
        safe_inc(s, se)
    se_exp8 = se.n

    @block.vector
    def _(v, g=g):
        wait(v, se, se_exp8)
        if H_["pe_states"]:
            wait(v, pe, H_["pe_states"][-1])
        v.tensor_tensor(B["r8b"].rearrange("p (a b) -> p a b", a=8),
                        ctx["m8"].rearrange("p (a b) -> p a b", a=8),
                        B["ed16"][0:8, None, :].broadcast_to([8, 8, 128]), ALU.mult)
        v.drain()
        v.tensor_scalar(B["cdm"][:], ctx["m8p"][:], B["cd8"][:], None, ALU.mult)
        safe_inc(v, ve)
    ve_pre = ve.n

    @block.tensor
    def _(te, g=g, sl=sl):
        wait(te, ve, ve_pre)
        if H_["se_exp"]:
            wait(te, se, H_["se_exp"][-1])
        for hf in range(2):
            sli = slice(hf * 512, (hf + 1) * 512)
            te.matmul(psD[:, sli], ctx["ones"][0:8, :], B["r8a"][:, sli],
                      start=True, stop=False)
            te.matmul(psD[:, sli], B["s8"][:, sl, 1, :], ctx["m8"][:, sli],
                      start=False, stop=False)
            for q in range(4):
                ins = te.matmul(psD[:, hf * 512 + q * 128: hf * 512 + (q + 1) * 128],
                                ctx["ident"][:], ctx["slm"][:],
                                start=False, stop=(q == 3))
        pe.inc(ins)
    pe_diff = pe.n

    @block.scalar
    def _(s, g=g):
        wait(s, pe, pe_diff)
        s.activation(B["mbuf"][:], psD[:], AF.Exp)
        safe_inc(s, se)
    se_exp = se.n

    @block.vector
    def _(v, g=g):
        wait(v, se, se_exp)
        wait(v, ve, ve_cb)
        v.tensor_mul(B["mbuf"].rearrange("p (a b) -> p a b", a=8),
                     B["mbuf"].rearrange("p (a b) -> p a b", a=8),
                     B["cb2"][:, None, :].broadcast_to([128, 8, 128]))
        safe_inc(v, ve)
    ve_m = ve.n

    @block.tensor
    def _(te, g=g):
        wait(te, ve, ve_pre)
        te.matmul(psP[:, 0, :], ctx["ones"][0:8, :], B["r8b"][:, 0:512],
                  start=True, stop=True)
        ins = te.matmul(psP[:, 1, :], ctx["ones"][0:8, :], B["r8b"][:, 512:1024],
                        start=True, stop=True)
        pe.inc(ins)
    pe_bcast = pe.n

    @block.vector
    def _(v, g=g):
        wait(v, pe, pe_bcast)
        C_fm = B["hbc"][:, 33, l0:l0 + SUB]
        v.tensor_mul(B["csc"][:, 0:512].rearrange("p (a b) -> p a b", a=4),
                     C_fm[:, None, :].broadcast_to([128, 4, 128]), psP[:, 0, :]
                     .rearrange("p (a b) -> p a b", a=4))
        v.drain()
        v.tensor_mul(B["csc"][:, 512:1024].rearrange("p (a b) -> p a b", a=4),
                     C_fm[:, None, :].broadcast_to([128, 4, 128]), psP[:, 1, :]
                     .rearrange("p (a b) -> p a b", a=4))
        safe_inc(v, ve)
    ve_cs = ve.n

    @block.tensor
    def _(te, g=g, sl=sl):
        wait(te, ve, ve_cs)
        for hl in range(8):
            h = g * 8 + hl
            te.matmul(psY[:, sl, hl * 64:(hl + 1) * 64],
                      B["mbuf"][:, hl * 128:(hl + 1) * 128],
                      B["xdt"][:, h * 64:(h + 1) * 64], start=True, stop=False)
            ins = te.matmul(psY[:, sl, hl * 64:(hl + 1) * 64],
                            B["csc"][:, hl * 128:(hl + 1) * 128],
                            B["carrybf"][:, h * 64:(h + 1) * 64], start=False, stop=True)
        pe.inc(ins)
    pe_y = pe.n

    @block.vector
    def _(v, g=g, sl=sl):
        wait(v, pe, pe_y)
        v.tensor_copy(Y_tm[:, g * 512:(g + 1) * 512], psY[:, sl, :])
        safe_inc(v, ve)
    ve_yev = ve.n

    @block.tensor
    def _(te, g=g):
        wait(te, ve, ve_cs)
        te.matmul(psP[:, 0, :], B_tm_sb[:], B["xdte"][:, g * 512:(g + 1) * 512],
                  start=True, stop=True)
        ins = te.matmul(psP[:, 1, :], ctx["ones"][0:8, :], B["cdm"][:],
                        start=True, stop=True)
        pe.inc(ins)
    pe_states = pe.n

    @block.vector
    def _(v, g=g):
        wait(v, pe, pe_states)
        cg = B["carry"][:, g * 512:(g + 1) * 512]
        v.tensor_mul(B["t2"].rearrange("p a b -> p (a b)")[:, 0:512], cg, psP[:, 1, :])
        v.drain()
        v.tensor_add(cg, B["t2"].rearrange("p a b -> p (a b)")[:, 0:512], psP[:, 0, :])
        v.drain()
        v.tensor_copy(B["carrybf"][:, g * 512:(g + 1) * 512], cg)
        safe_inc(v, ve)
    ve_carry = ve.n

    for k_, v_ in (("pe_diff", pe_diff), ("pe_bcast", pe_bcast), ("pe_y", pe_y),
                   ("pe_states", pe_states), ("ve_pre", ve_pre), ("ve_cs", ve_cs),
                   ("ve_carry", ve_carry), ("ve_yev", ve_yev), ("se_exp", se_exp)):
        H_[k_].append(v_)


def _mamba_epilogue(ctx, B, t0):
    nc, block, SM = ctx["nc"], ctx["block"], ctx["SM"]
    pe, se, ve, po = SM["pe"], SM["se"], SM["ve"], SM["po"]
    dw = [SM["dwa0"], SM["dwa1"]]
    dsl = [SM["dx0"], SM["dx1"]]
    dout = SM["dout"]
    T = MAM_T
    xfm = ctx["xfm"]
    yfm, gsil, hbc = B["yfm"], B["gsil"], B["hbc"]
    psP, psY = B["psP"], B["psY"]
    psRS = psY[0:1, 0, 0:T]
    psB2 = psY[:, 1, 0:T]

    ve_yg = [0] * 32
    for ci in range(32):
        @block.vector
        def _(v, ci=ci):
            if ci == 0:
                wait(v, se, se.n)
                wait(v, ve, ve.n)
            v.scalar_tensor_tensor(B["t2"][:, ci % 2, :], hbc[:, ci, :],
                                   ctx["dcol"][:, ci:ci + 1], yfm[:, ci, :],
                                   ALU.mult, ALU.add)
            v.drain()
            v.tensor_mul(yfm[:, ci, :], B["t2"][:, ci % 2, :], gsil[:, ci, :])
            ve_yg[ci] = safe_inc(v, ve)

    se_sq = [0] * 32
    for ci in range(32):
        @block.scalar
        def _(s, ci=ci):
            wait(s, ve, ve_yg[ci])
            if ci >= 1:
                wait(s, pe, pe.n)
            se_sq[ci] = se.inc(s.activation(B["xsq"][:], yfm[:, ci, :], AF.Square))

        @block.tensor
        def _(te, ci=ci):
            wait(te, se, se_sq[ci])
            if ci == 0:
                wait(te, ve, ve.n)
            ins = te.matmul(psRS[:], ctx["ones_bf"][:, 0:1], B["xsq"][:],
                            start=(ci == 0), stop=(ci == 31))
            pe.inc(ins)

    rows = B["rows"]

    @block.vector
    def _(v):
        wait(v, pe, pe.n)
        v.tensor_scalar_mul(rows[:, 0, :], psRS[:], 1.0 / I_M)
        safe_inc(v, ve)

    @block.scalar
    def _(s):
        wait(s, ve, ve.n)
        s.activation(rows[:, 1, :], rows[:, 0, :], AF.Ln, bias=EPS)
        s.drain()
        s.activation(rows[:, 0, :], rows[:, 1, :], AF.Exp, scale=-0.5)
        safe_inc(s, se)

    @block.tensor
    def _(te):
        wait(te, se, se.n)
        pe.inc(te.matmul(psB2[:], ctx["ones"][0:1, :], rows[:, 0, :],
                         start=True, stop=True))

    ve_yn = [0] * 32
    for ci in range(32):
        @block.vector
        def _(v, ci=ci):
            wait(v, pe, pe.n)
            v.tensor_mul(B["t2"][:, ci % 2, :], yfm[:, ci, :], psB2[:])
            v.drain()
            v.tensor_scalar(gsil[:, ci, :], B["t2"][:, ci % 2, :],
                            ctx["nwpp"][:, ci:ci + 1], None, ALU.mult)
            ve_yn[ci] = safe_inc(v, ve)

    pe_mo = [0] * 16
    ve_res = [0] * 16
    dout_at = [0] * 16
    for mo in range(16):
        sl = mo % 2

        @block.sync
        def _(sy, mo=mo, sl=sl):
            if mo >= 2:
                wait(sy, pe, pe_mo[mo - 2])
            else:
                wait(sy, pe, pe.n)
            dw[sl].inc(sy.dma_start(out=B["wband"][:, sl, :, :],
                                    in_=ctx["wout_s"][mo]), 16)

        @block.sync
        def _(sy, mo=mo, sl=sl):
            if mo >= 2:
                wait(sy, ve, ve_res[mo - 2])
            dsl[sl].inc(sy.dma_start(out=B["xbuf"][:, sl, :],
                                     in_=xfm[mo * 128:(mo + 1) * 128, t0:t0 + T]), 16)

        @block.tensor
        def _(te, mo=mo, sl=sl):
            wait(te, dw[sl], dw[sl].n)
            if mo == 0:
                wait(te, ve, ve_yn[31])
            if mo >= 2:
                wait(te, ve, ve_res[mo - 2])
            ins = None
            for k in range(32):
                ins = te.matmul(psP[:, sl, 0:T], B["wband"][:, sl, k, :],
                                gsil[:, k, :], start=(k == 0), stop=(k == 31))
            pe_mo[mo] = pe.inc(ins)

        @block.vector
        def _(v, mo=mo, sl=sl):
            wait(v, pe, pe_mo[mo])
            wait(v, dsl[sl], dsl[sl].n)
            if mo >= 2:
                wait(v, dout, dout_at[mo - 2])
            ve_res[mo] = ve.inc(v.tensor_add(B["t2"][:, sl, :], psP[:, sl, 0:T],
                                             B["xbuf"][:, sl, :]))

        @block.gpsimd
        def _(gp, mo=mo, sl=sl):
            wait(gp, ve, ve_res[mo])
            dout.inc(gp.dma_start(out=xfm[mo * 128:(mo + 1) * 128, t0:t0 + T],
                                  in_=B["t2"][:, sl, :]), 16)
            dout_at[mo] = dout.n

    @block.gpsimd
    def _(gp):
        wait(gp, dout, dout.n)


def emit_post(ctx):
    nc, block, SM = ctx["nc"], ctx["block"], ctx["SM"]
    pe, se, ve = SM["pe"], SM["se"], SM["ve"]
    dsl = [SM["dx0"], SM["dx1"]]
    dout = SM["dout"]
    S = ctx["S"]
    with nc.sbuf_tensor("po_in", [128, 2, 128], F32) as xin, \
         nc.sbuf_tensor("po_ot", [128, 2, HID], F32) as otm, \
         nc.psum_tensor("po_ps", [128, 2, 512], F32) as pst:
        dout_at = {}
        pe_tr = {}
        se_cp = {}
        for tb in range(S // 128):
            for fc in range(NH):
                sl = fc % 2
                idx = tb * NH + fc

                @block.sync
                def _(sy, tb=tb, fc=fc, sl=sl, idx=idx):
                    if idx >= 2:
                        wait(sy, pe, pe_tr[idx - 2])
                    dsl[sl].inc(sy.dma_start(
                        out=xin[:, sl, :],
                        in_=ctx["xfm"][fc * 128:(fc + 1) * 128, tb * 128:(tb + 1) * 128]), 16)

                @block.tensor
                def _(te, fc=fc, sl=sl, idx=idx):
                    wait(te, dsl[sl], dsl[sl].n)
                    if idx >= 2:
                        wait(te, se, se_cp[idx - 2])
                    else:
                        wait(te, se, se.n)
                    pe_tr[idx] = pe.inc(te.transpose(pst[:, sl, 0:128], xin[:, sl, :],
                                                     ctx["ident"][:]))

                @block.scalar
                def _(s, tb=tb, fc=fc, sl=sl, idx=idx):
                    wait(s, pe, pe_tr[idx])
                    if fc == 0 and tb >= 2:
                        wait(s, dout, dout_at[tb - 2])
                    se_cp[idx] = se.inc(s.copy(otm[:, tb % 2, fc * 128:(fc + 1) * 128],
                                               pst[:, sl, 0:128]))

            @block.gpsimd
            def _(gp, tb=tb):
                wait(gp, se, se_cp[tb * NH + NH - 1])
                dout.inc(gp.dma_start(out=ctx["out"][tb * 128:(tb + 1) * 128, :],
                                      in_=otm[:, tb % 2, :]), 16)
                dout_at[tb] = dout.n

        @block.gpsimd
        def _(gp):
            wait(gp, dout, dout.n)


def kernel(L0_attn_x, L0_attn_W_in, L0_attn_conv_w, L0_attn_conv_b,
           L0_attn_dt_bias, L0_attn_A_log, L0_attn_D, L0_attn_norm_w,
           L0_attn_W_out, L0_ffn_gate_w, L0_ffn_w_gate_up, L0_ffn_w_down,
           L0_ffn_shared_gate_up, L0_ffn_shared_down):
    Bn, S, _ = L0_attn_x.shape
    assert Bn == 8
    nc = build_program(S)
    shared = {
        "w_in": np.ascontiguousarray(L0_attn_W_in),
        "conv_w": np.ascontiguousarray(L0_attn_conv_w),
        "conv_b": np.ascontiguousarray(L0_attn_conv_b),
        "dt_bias": np.ascontiguousarray(L0_attn_dt_bias),
        "a_log": np.ascontiguousarray(L0_attn_A_log),
        "d_vec": np.ascontiguousarray(L0_attn_D),
        "norm_w": np.ascontiguousarray(L0_attn_norm_w),
        "w_out": np.ascontiguousarray(L0_attn_W_out),
        "gate_w": np.ascontiguousarray(L0_ffn_gate_w),
        "w_gate_up": np.ascontiguousarray(L0_ffn_w_gate_up),
        "w_down": np.ascontiguousarray(L0_ffn_w_down),
        "shared_gate_up": np.ascontiguousarray(L0_ffn_shared_gate_up),
        "shared_down": np.ascontiguousarray(L0_ffn_shared_down),
    }
    in_maps = [dict(shared, x=np.ascontiguousarray(L0_attn_x[c]))
               for c in range(8)]
    res = run_bass_kernel_spmd(nc, in_maps, core_ids=list(range(8)))
    return np.stack([res.results[c]["out"] for c in range(8)], axis=0)
